# version 27 (final)
# baseline (speedup 1.0000x reference)
import numpy as np
import ml_dtypes
from contextlib import ExitStack

import concourse.bass as bass
import concourse.mybir as mybir
from concourse.bass_utils import run_bass_kernel_spmd

F32 = mybir.dt.float32
BF16 = mybir.dt.bfloat16
AF = mybir.ActivationFunctionType
ALU = mybir.AluOpType
AX = mybir.AxisListType


class Cfg:
    def __init__(self, H=2048, I=1024, TC=1024, E=16, n_cores=8,
                 n_group=4, topk=4, topk_group=2, scaling=2.5):
        self.H, self.I, self.TC, self.E, self.n_cores = H, I, TC, E, n_cores
        self.n_group, self.topk, self.topk_group, self.scaling = n_group, topk, topk_group, scaling
        self.HB = H // 128
        self.IB = I // 128
        self.TB = TC // 128
        self.HC = H // 512
        self.TCH = TC // 512
        self.gsz = E // n_group
        self.EBC = min(H, 1024)
        self.NEH = H // self.EBC
        assert H % 512 == 0 and I % 128 == 0 and TC % 512 == 0 and self.HC <= 4


CFG = Cfg()

_ENG2BLK = {"te": "tensor", "vt": "vector", "sc": "scalar", "gp": "gpsimd", "sy": "sync"}


class Buf:
    __slots__ = ("w", "rs", "name")

    def __init__(self, name=""):
        self.w = None
        self.rs = []
        self.name = name

    def deps(self):
        return ([self.w] if self.w is not None else []) + list(self.rs)


class Prog:
    def __init__(self, nc):
        self.nc = nc
        self.ops = {e: [] for e in _ENG2BLK}
        self.cnt = {}
        self.sems = {}

    def _tok(self, sem, inc):
        v = self.cnt.get(sem, 0) + inc
        self.cnt[sem] = v
        return (sem, v)

    def op(self, eng, emit, reads=(), writes=(), extra=(), war=(), inc=1, sem=None, skip_sem=None):
        sem = sem or eng
        deps = [t for t in extra if t is not None]
        for b in reads:
            if b.w is not None:
                deps.append(b.w)
        for b in writes:
            deps.extend(b.deps())
        for b in war:
            deps.extend(b.deps())
        if skip_sem is not None:
            deps = [t for t in deps if t[0] != skip_sem]
        tok = self._tok(sem, inc) if inc else None
        self.ops[eng].append((tuple(deps), emit, sem if inc else None, inc))
        if tok is not None:
            for b in reads:
                b.rs.append(tok)
            for b in writes:
                b.w = tok
                b.rs = []
        return tok

    def dma(self, eng, out, in_, group=None, reads=(), writes=(), extra=(), **kw):
        g = "d_" + (group or eng)
        return self.op(eng,
                       (lambda e, out=out, in_=in_, kw=kw: e.dma_start(out=out, in_=in_, **kw)),
                       reads=reads, writes=writes, extra=extra, inc=16, sem=g, skip_sem=g)

    def wait(self, eng, tok):
        self.ops[eng].append(((tok,), None, None, 0))

    def emit(self, block, es):
        for name in self.cnt:
            self.sems[name] = es.enter_context(self.nc.semaphore(name))
        for eng, blkname in _ENG2BLK.items():
            ops = self.ops[eng]
            if not ops:
                continue

            def body(e, ops=ops):
                hw = {}
                for deps, emitfn, sem, inc in ops:
                    for (s, v) in deps:
                        if hw.get(s, 0) < v:
                            e.wait_ge(self.sems[s], v)
                            hw[s] = v
                    if emitfn is None:
                        continue
                    ins = emitfn(e)
                    if inc:
                        ins.then_inc(self.sems[sem], inc)
            getattr(block, blkname)(body)


def build_nc(cfg):
    H, I, TC, E = cfg.H, cfg.I, cfg.TC, cfg.E
    HB, IB, TB, HC, TCH = cfg.HB, cfg.IB, cfg.TB, cfg.HC, cfg.TCH
    NG, TK, TKG, gsz = cfg.n_group, cfg.topk, cfg.topk_group, cfg.gsz
    EBC, NEH = cfg.EBC, cfg.NEH
    NE = E + 1

    nc = bass.Bass("TRN2", target_bir_lowering=False, debug=False)

    x_d = nc.dram_tensor("x", [TC, H], F32, kind="ExternalInput").ap()
    gup_d = nc.dram_tensor("gup", [NE, IB, 2, 128, HB * 128], F32, kind="ExternalInput").ap()
    dnt_d = nc.dram_tensor("dnt", [NE, IB, 128, H], F32, kind="ExternalInput").ap()
    wt_d = nc.dram_tensor("wt", [HB, 128, E], F32, kind="ExternalInput").ap()
    eb_d = nc.dram_tensor("eb", [1, E], F32, kind="ExternalInput").ap()
    ones1_d = nc.dram_tensor("ones1", [1, 128], F32, kind="ExternalInput").ap()
    ones128_d = nc.dram_tensor("ones128", [128, 128], BF16, kind="ExternalInput").ap()
    ident_d = nc.dram_tensor("ident", [128, 128], F32, kind="ExternalInput").ap()
    out_d = nc.dram_tensor("out", [TC, H], F32, kind="ExternalOutput").ap()
    xn3_d = nc.dram_tensor("xn3", [HB, TC, 128], BF16, kind="Internal").ap()
    lo3_d = nc.dram_tensor("lo3", [HB, TC, 128], BF16, kind="Internal").ap()
    gpart_d = nc.dram_tensor("gpart", [H, H], BF16, kind="Internal").ap()
    gfull_shared = cfg.n_cores > 4
    gfull_d = nc.dram_tensor("gfull", [H, H], BF16, kind="Internal",
                             addr_space=("Shared" if gfull_shared else "Local")).ap()
    tt_d = nc.dram_tensor("ttd", [HB, 128, H], BF16, kind="Internal").ap()
    cwt_d = nc.dram_tensor("cwtd", [E, TC], F32, kind="Internal").ap()
    negm_d = nc.dram_tensor("negmd", [1, H], F32, kind="Internal").ap()

    es = ExitStack()
    sb = lambda n, s, d: es.enter_context(nc.sbuf_tensor(n, s, d))

    acc = sb("acc", [128, TB * H], F32)
    xt = sb("xt", [128, HB * TC], BF16)
    xnb = sb("xnb", [128, max(TB * H, IB * TC + 2 * H + 2 * TC)], BF16)
    wbig = sb("wbig", [128, max(IB * H, 2 * HB * 512)], BF16)
    wgu = sb("wgu", [128, 6 * H], BF16)
    fbE = sb("fbE", [128, 512], F32)
    fbM = sb("fbM", [128, H], F32)
    loscr = sb("loscr", [128, max(EBC, TCH * 512)], BF16)
    ebb = sb("ebb", [128, 2 * EBC], BF16)
    lgt = sb("lgt", [128, 128], F32)
    r1 = sb("r1", [128, 128], F32)
    r2 = sb("r2", [128, 128], F32)
    r3 = sb("r3", [128, 128], F32)
    r4 = sb("r4", [128, 128], F32)
    gsb = sb("gsb", [128, 96], F32)
    statsT = r1[0:16, 0:128]
    stats = sb("stats", [128, 16], F32)
    w16 = sb("w16", [128, 2 * HB * E], BF16)
    ident = sb("ident_sb", [128, 128], F32)
    ones1 = sb("ones1_sb", [1, 128], F32)
    ones128 = sb("ones128_sb", [128, 128], BF16)
    ebias = sb("ebias_sb", [1, E], F32)
    bias_bc = sb("bias_bc", [128, E], F32)
    row = sb("row", [1, 512], F32)
    ln = sb("ln", [128, 4 * TB + 1], F32)
    bnst = sb("bnst", [128, 6 * 4], F32)
    gpad = bnst[:, 0:8]
    m8 = bnst[:, 8:16]

    psb = [es.enter_context(nc.psum_tensor(f"pb{i}", [128, 512], F32)) for i in range(8)]

    acc_t = [acc[:, t * H:(t + 1) * H] for t in range(TB)]
    xt_h = [xt[:, h * TC:(h + 1) * TC] for h in range(HB)]
    xn_t = [xnb[:, t * H:(t + 1) * H] for t in range(TB)]
    h_i = [xnb[:, i * TC:(i + 1) * TC] for i in range(IB)]
    tbuf = [xnb[:, IB * TC + k * H: IB * TC + (k + 1) * H] for k in range(2)]
    cwb = [xnb[:, IB * TC + 2 * H + k * TC: IB * TC + 2 * H + (k + 1) * TC] for k in range(2)]
    lostr = cwb
    dt_i = [wbig[:, i * H:(i + 1) * H] for i in range(IB)]
    ttr_w = HB * 512
    ttr = [wbig[:, k * ttr_w:(k + 1) * ttr_w] for k in range(2)]
    whi = [w16[:, hb * E:(hb + 1) * E] for hb in range(HB)]
    wlo = [w16[:, (HB + hb) * E:(HB + hb + 1) * E] for hb in range(HB)]

    p = Prog(nc)

    bacc = [Buf(f"acc{t}") for t in range(TB)]
    bxt = [Buf(f"xt{h}") for h in range(HB)]
    bxn = [Buf(f"xn{t}") for t in range(TB)]
    bh = [Buf(f"h{i}") for i in range(IB)]
    btb = [Buf(f"tb{k}") for k in range(2)]
    bcw = [Buf(f"cw{k}") for k in range(2)]
    bdt = Buf("dt")
    bttr = [Buf(f"ttr{k}") for k in range(2)]
    bwg = [Buf(f"wg{k}") for k in range(3)]
    bwu = [Buf(f"wu{k}") for k in range(3)]
    bE = Buf("fbE")
    bebb = [Buf("ebb0"), Buf("ebb1")]
    bM = Buf("fbM")
    bD = Buf("fbD")
    blo = Buf("loscr")
    bpb = [Buf(f"pb{i}") for i in range(8)]
    blgt, br1, br2, br3, br4 = Buf("lgt"), Buf("r1"), Buf("r2"), Buf("r3"), Buf("r4")
    bgs = Buf("gsb")
    bgpad = Buf("gpad")
    bm8 = Buf("m8")
    bstats, bstatsT = Buf("stats"), Buf("statsT")
    bw16 = Buf("w16")
    bconst, bebias, bbias = Buf("const"), Buf("ebias"), Buf("bias_bc")
    brow = Buf("row")
    brow2 = Buf("row2")
    bln = Buf("ln")
    bbn = Buf("bnst")
    bln_t = [Buf(f"ln{t}") for t in range(TB)]
    bxn3, blo3 = Buf("xn3"), Buf("lo3")
    bgpart, bgfull = Buf("gpart"), Buf("gfull")
    btt_d = [Buf(f"ttd{j}") for j in range(HB)]
    bcwt_d = Buf("cwtd")
    bnegm_d = Buf("negmd")

    NSLOT = 3
    wg_view = [wgu[:, (2 * k) * H:(2 * k + 1) * H] for k in range(NSLOT)]
    wu_view = [wgu[:, (2 * k + 1) * H:(2 * k + 2) * H] for k in range(NSLOT)]
    wpair_view = [wgu[:, (2 * k) * H:(2 * k + 2) * H] for k in range(NSLOT)]

    for t in range(TB):
        p.dma("sy", acc_t[t], x_d[t * 128:(t + 1) * 128, :], group="ac%d" % (0 if t < TB // 2 else 1),
              writes=[bacc[t]])
    p.dma("sy", ident[:], ident_d, group="cst", writes=[bconst])
    p.dma("sy", ones1[:], ones1_d, group="cst")
    p.dma("sy", ones128[:], ones128_d, group="cst")
    tok_c = p.dma("sy", ebias[:], eb_d, group="cst", writes=[bebias])
    bconst.w = tok_c
    bebias.w = tok_c
    for t in range(TB):
        bacc[t].w = bacc[(TB // 2 - 1) if t < TB // 2 else (TB - 1)].w
    p.dma("gp", w16[:, 0:HB * E].rearrange("p (b e) -> p b e", b=HB),
          wt_d.rearrange("b p e -> p b e"), group="w16", writes=[bw16])
    p.op("vt", lambda e: e.memset(ln[:, 4 * TB:4 * TB + 1], 1e-5), writes=[bln])
    for hb in range(HB):
        p.dma("sy", stats[:, 0:E], wt_d[hb], group="wst", writes=[bstats])
        p.op("vt", lambda e, hb=hb: e.tensor_tensor(wlo[hb], stats[:, 0:E], whi[hb], ALU.subtract),
             reads=[bstats, bw16], writes=[bw16])

    NH = H // EBC
    fbD_v = fbM[:, 0:EBC]
    fbD_hi = fbM[:, EBC:2 * EBC] if NH > 1 else fbM[:, 0:EBC]

    def rec_lo(t):
        c0 = 4 * t
        nmr, rstd = ln[:, c0 + 3:c0 + 4], ln[:, c0 + 2:c0 + 3]
        for hh in range(NH):
            sl = slice(hh * EBC, (hh + 1) * EBC)
            buf, bbuf = (fbD_hi, bE) if hh % 2 == 0 else (fbD_v, bD)
            p.op("gp", lambda e, t=t, sl=sl, buf=buf, o=nmr, r=rstd:
                 e.tensor_scalar(buf, acc_t[t][:, sl], o, r, ALU.add, ALU.mult),
                 reads=[bacc[t], bln_t[t]], writes=[bbuf])
            p.op("gp", lambda e, t=t, sl=sl, buf=buf: e.tensor_tensor(loscr[:, 0:EBC], buf, xn_t[t][:, sl],
                                                                      ALU.subtract),
                 reads=[bbuf, bxn[t]], writes=[blo])
            hb0 = hh * (EBC // 128)
            nhb = EBC // 128
            p.dma("gp", lo3_d[hb0:hb0 + nhb, t * 128:(t + 1) * 128, :].rearrange("b t h -> t b h"),
                  loscr[:, 0:EBC].rearrange("t (b h) -> t b h", b=nhb), group="lo3", reads=[blo], writes=[blo3])

    for t in range(TB):
        c0 = 4 * t
        mean, var, rstd, nmr = ln[:, c0:c0 + 1], ln[:, c0 + 1:c0 + 2], ln[:, c0 + 2:c0 + 3], ln[:, c0 + 3:c0 + 4]
        for c4 in range(HC):
            p.op("vt", lambda e, t=t, c4=c4: e.bn_stats(bnst[:, 6 * c4:6 * (c4 + 1)],
                                                        acc_t[t][:, 512 * c4:512 * (c4 + 1)]),
                 reads=[bacc[t]], writes=[bbn])
        p.op("vt", lambda e, t=t, c0=c0: e.bn_aggr(ln[:, c0:c0 + 2], bnst[:, 0:6 * HC]),
             reads=[bbn], writes=[bln_t[t]])
        p.op("sc", lambda e, v=var, r=rstd: e.activation(r, v, AF.Sqrt, bias=ln[:, 4 * TB:4 * TB + 1]),
             reads=[bln_t[t]], extra=[bln.w], writes=[bln_t[t]])
        p.op("vt", lambda e, r=rstd: e.reciprocal(r, r), reads=[bln_t[t]], writes=[bln_t[t]])
        p.op("vt", lambda e, m=mean, o=nmr: e.tensor_scalar(o, m, -1.0, None, ALU.mult),
             reads=[bln_t[t]], writes=[bln_t[t]])
        for hh in range(NH):
            sl = slice(hh * EBC, (hh + 1) * EBC)
            eng = "vt" if hh % 2 == 0 else "gp"
            p.op(eng, lambda e, t=t, sl=sl, o=nmr, r=rstd:
                 e.tensor_scalar(xn_t[t][:, sl], acc_t[t][:, sl], o, r, ALU.add, ALU.mult),
                 reads=[bacc[t], bln_t[t]], writes=[bxn[t]])
        p.dma("sy", xn3_d[:, t * 128:(t + 1) * 128, :].rearrange("b t h -> t b h"),
              xn_t[t].rearrange("t (b h) -> t b h", b=HB), group="xn3", reads=[bxn[t]], writes=[bxn3])
    for t in range(TB):
        rec_lo(t)

    for hb in range(HB):
        p.dma("sc", xt_h[hb], xn3_d[hb], group="trx", extra=[bxn3.w], writes=[bxt[hb]], transpose=True)
    for hb in range(HB):
        bxt[hb].w = bxt[HB - 1].w


    def rec_router():
        lps, blps = [psb[0], psb[1]][:TCH], [bpb[0], bpb[1]][:TCH]
        nrm = [0]
        total_rmm = 3 * HB

        def rmm(lw, lw_bufs, rhs, rhs_bufs):
            k = nrm[0]
            nrm[0] += 1
            p.op("te", lambda e, lw=lw: e.ldweights(lw), extra=[b.w for b in lw_bufs], inc=0)
            for c in range(TCH):
                first, lastm = (k == 0), (k == total_rmm - 1)
                p.op("te", lambda e, c=c, lw=lw, rhs=rhs, f=first, l=lastm: e.matmul(
                    lps[c][0:E, :], lw, rhs[:, c * 512:(c + 1) * 512], start=f, stop=l),
                    reads=list(rhs_bufs) + list(lw_bufs),
                    writes=([blps[c]] if (first or lastm) else []), inc=1)

        for hb in range(min(2, HB)):
            p.dma("sc", lostr[hb % 2], lo3_d[hb], group="tl%d" % (hb % 2), extra=[blo3.w],
                  writes=[bcw[hb % 2]], transpose=True)
        for hb in range(HB):
            rmm(whi[hb], [bw16], xt_h[hb], [bxt[hb]])
            rmm(wlo[hb], [bw16], xt_h[hb], [bxt[hb]])
        for hb in range(HB):
            rmm(whi[hb], [bw16], lostr[hb % 2], [bcw[hb % 2]])
            if hb + 2 < HB:
                p.dma("sc", lostr[(hb + 2) % 2], lo3_d[hb + 2], group="tl%d" % ((hb + 2) % 2),
                      extra=[blo3.w], writes=[bcw[(hb + 2) % 2]], transpose=True)
        for tb in range(TB):
            c, off = (tb * 128) // 512, (tb * 128) % 512
            p.op("vt", lambda e, c=c, off=off: e.tensor_copy(statsT, lps[c][0:E, off:off + 128]),
                 reads=[blps[c]], writes=[bstatsT])
            p.op("te", lambda e: e.transpose(psb[2][0:128, 0:E], statsT, ident[0:E, 0:E]),
                 reads=[bstatsT], extra=[bconst.w], writes=[bpb[2]], inc=1)
            p.op("vt", lambda e, tb=tb: e.tensor_copy(lgt[:, tb * E:(tb + 1) * E], psb[2][0:128, 0:E]),
                 reads=[bpb[2]], writes=[blgt])
        p.op("te", lambda e: e.matmul(psb[3][:, 0:E], ones1[:], ebias[:], start=True, stop=True),
             reads=[bebias], writes=[bpb[3]], extra=[bconst.w], inc=1)
        p.op("vt", lambda e: e.tensor_copy(bias_bc[:], psb[3][:, 0:E]), reads=[bpb[3]], writes=[bbias])

        for _b in (bgpad, bm8):
            _b.w = bbn.w
            _b.rs = list(bbn.rs)
        p.op("vt", lambda e: e.memset(gpad[:, 4:8], -1e30), writes=[bgpad])
        br1.rs.extend(bstatsT.rs)
        if bstatsT.w is not None:
            br1.rs.append(bstatsT.w)
        NBE = TB * E
        p.op("sc", lambda e: e.activation(r1[:, 0:NBE], lgt[:, 0:NBE], AF.Sigmoid), reads=[blgt], writes=[br1])
        for tb in range(TB):
            p.op("vt", lambda e, tb=tb: e.tensor_tensor(r2[:, tb * E:(tb + 1) * E], r1[:, tb * E:(tb + 1) * E],
                                                        bias_bc[:], ALU.add),
                 reads=[br1, bbias], writes=[br2])
        NBG = TB * NG
        b4 = r2[:, 0:NBE].rearrange("p (q s) -> p q s", s=gsz)
        gv = [gsb[:, k * 32:k * 32 + NBG] for k in range(3)]
        pairs = [(a, b) for a in range(gsz) for b in range(a + 1, gsz)]
        for n, (a, b) in enumerate(pairs):
            dst = gv[0] if n == 0 else gv[1]
            p.op("vt", lambda e, a=a, b=b, dst=dst: e.tensor_tensor(dst, b4[:, :, a], b4[:, :, b], ALU.add),
                 reads=[br2], writes=[bgs])
            if n > 0:
                p.op("vt", lambda e: e.tensor_tensor(gv[0], gv[0], gv[1], ALU.max), reads=[bgs], writes=[bgs])
        for tb in range(TB):
            sl = slice(tb * NG, (tb + 1) * NG)
            p.op("vt", lambda e, sl=sl: e.tensor_copy(gpad[:, 0:NG], gv[0][:, sl]),
                 reads=[bgs], writes=[bgpad])
            p.op("vt", lambda e: e.max(out=m8, in_=gpad), reads=[bgpad], writes=[bm8])
            p.op("vt", lambda e: e.memset(m8[:, TKG:8], -1e30), writes=[bm8])
            p.op("vt", lambda e: e.match_replace(gpad, m8, gpad, -1e30),
                 reads=[bm8], writes=[bgpad])
            p.op("vt", lambda e, sl=sl: e.tensor_copy(gv[1][:, sl], gpad[:, 0:NG]),
                 reads=[bgpad], writes=[bgs])
        p.op("vt", lambda e: e.tensor_tensor(gv[2], gv[0], gv[1], ALU.not_equal), reads=[bgs], writes=[bgs])
        for q in range(NBG):
            p.op("vt", lambda e, q=q: e.tensor_scalar(r3[:, q * gsz:(q + 1) * gsz], ones128[0:128, 0:gsz],
                                                      gv[2][:, q:q + 1], None, ALU.mult),
                 reads=[bgs], extra=[bconst.w], writes=[br3])
        p.op("vt", lambda e: e.tensor_scalar(lgt[:, 0:NBE], r3[:, 0:NBE], 1e30, -1e30, ALU.mult, ALU.add),
             reads=[br3], writes=[blgt])
        p.op("vt", lambda e: e.tensor_tensor(r4[:, 0:NBE], r2[:, 0:NBE], r3[:, 0:NBE], ALU.mult),
             reads=[br2, br3], writes=[br4])
        p.op("vt", lambda e: e.tensor_tensor(r4[:, 0:NBE], r4[:, 0:NBE], lgt[:, 0:NBE], ALU.add),
             reads=[blgt], writes=[br4])
        p.op("vt", lambda e: e.tensor_copy(lgt[:, 0:NBE], r4[:, 0:NBE]), reads=[br4], writes=[blgt])
        for tb in range(TB):
            sl = slice(tb * E, (tb + 1) * E)
            p.op("vt", lambda e, sl=sl: e.max(out=m8, in_=lgt[:, sl]), reads=[blgt], writes=[bm8])
            p.op("vt", lambda e: e.memset(m8[:, TK:8], -2e30), writes=[bm8])
            p.op("vt", lambda e, sl=sl: e.match_replace(lgt[:, sl], m8, lgt[:, sl], -2e30),
                 reads=[bm8], writes=[blgt])
        p.op("vt", lambda e: e.tensor_tensor(r3[:, 0:NBE], r4[:, 0:NBE], lgt[:, 0:NBE], ALU.not_equal),
             reads=[br4, blgt], writes=[br3])
        p.op("vt", lambda e: e.tensor_tensor(r4[:, 0:NBE], r1[:, 0:NBE], r3[:, 0:NBE], ALU.mult),
             reads=[br1, br3], writes=[br4])
        p.op("vt", lambda e: e.reduce_sum(gv[0][:, 0:TB], r4[:, 0:NBE].rearrange("p (b s) -> p b s", s=E),
                                          axis=AX.X), reads=[br4], writes=[bgs])
        p.op("vt", lambda e: e.tensor_scalar(gv[0][:, 0:TB], gv[0][:, 0:TB], 1e-20, None, ALU.add),
             reads=[bgs], writes=[bgs])
        p.op("vt", lambda e: e.reciprocal(gv[0][:, 0:TB], gv[0][:, 0:TB]), reads=[bgs], writes=[bgs])
        for tb in range(TB):
            p.op("vt", lambda e, tb=tb: e.tensor_scalar(r4[:, tb * E:(tb + 1) * E], r4[:, tb * E:(tb + 1) * E],
                                                        gv[0][:, tb:tb + 1], cfg.scaling, ALU.mult, ALU.mult),
                 reads=[bgs], writes=[br4])

    last_gram = None
    for it in range(HB):
        s0 = (it % 2) * 4
        banks = [psb[s0 + c] for c in range(HC)]
        bbs = [bpb[s0 + c] for c in range(HC)]
        for t in range(TB):
            p.op("te", lambda e, t=t, it=it: e.ldweights(xn_t[t][:, it * 128:(it + 1) * 128]),
                 extra=[bxn[t].w], inc=0)
            first, lastm = (t == 0), (t == TB - 1)
            for c in range(HC):
                last_gram = p.op("te", lambda e, t=t, c=c, it=it, bank=banks[c], f=first, l=lastm:
                                 e.matmul(bank[:], xn_t[t][:, it * 128:(it + 1) * 128],
                                          xn_t[t][:, c * 512:(c + 1) * 512], start=f, stop=l),
                                 reads=([bxn[t]] if lastm else []),
                                 writes=([bbs[c]] if lastm else []),
                                 war=([bbs[c]] if first else []),
                                 extra=([bxn[t].w] if not lastm else []),
                                 inc=(1 if lastm else 0))
        for c in range(HC):
            p.op("vt", lambda e, c=c, bank=banks[c]: e.tensor_copy(ebb[:, c * 512:(c + 1) * 512], bank[:]),
                 reads=[bbs[c]], writes=[bebb[c // 2]])
        p.dma("sy", gpart_d[it * 128:(it + 1) * 128, 0:min(H, 2048)], ebb[:, 0:min(H, 2048)],
              group="gr0", reads=[bebb[0], bebb[1]], writes=[bgpart])
    for t in range(TB):
        bxn[t].rs.append(last_gram)
    for k in range(2):
        bcw[k].rs.append(last_gram)
        btb[k].rs.append(last_gram)
    for i in range(IB):
        bh[i].rs.append(last_gram)
    rec_router()

    def rec_cwt():
        bstatsT.w = br1.w
        bstatsT.rs = list(br1.rs)
        for tb in range(TB):
            p.op("te", lambda e, tb=tb: e.transpose(psb[2][0:E, 0:128], r4[:, tb * E:(tb + 1) * E], ident[:]),
                 reads=[br4], writes=[bpb[2]], extra=[bconst.w], inc=1)
            p.op("vt", lambda e: e.tensor_copy(statsT, psb[2][0:E, 0:128]),
                 reads=[bpb[2]], writes=[bstatsT])
            p.dma("sy", cwt_d[:, tb * 128:(tb + 1) * 128], statsT, group="cwt",
                  reads=[bstatsT], writes=[bcwt_d])
    rt = bcw[0].rs + bcw[1].rs + ([bcw[0].w] if bcw[0].w else []) + ([bcw[1].w] if bcw[1].w else [])
    for k in range(2):
        bcw[k].w = None
        bcw[k].rs = list(rt)

    dnt_toks = {}

    ebufs = [(ebb[:, 0:EBC], bebb[0]), (ebb[:, EBC:2 * EBC], bebb[1])]

    def rec_stats_dve():
        units = []
        n = 0
        for it in range(HB):
            for hh in range(NEH):
                def unit(it=it, hh=hh, n=n):
                    buf, bbuf = ebufs[n % 2]
                    p.dma("sy", buf, gfull_d[it * 128:(it + 1) * 128, hh * EBC:(hh + 1) * EBC],
                          group="er%d" % (n % 2), extra=[bgfull.w], writes=[bbuf])
                    if hh == 0:
                        p.op("vt", lambda e, it=it, buf=buf: e.reduce_max(stats[:, it:it + 1], buf, axis=AX.X),
                             reads=[bbuf], writes=[bstats])
                    else:
                        p.op("vt", lambda e, buf=buf: e.reduce_max(m8[:, 0:1], buf, axis=AX.X),
                             reads=[bbuf], writes=[bm8])
                        p.op("vt", lambda e, it=it: e.tensor_tensor(stats[:, it:it + 1], stats[:, it:it + 1],
                                                                    m8[:, 0:1], ALU.max),
                             reads=[bm8], writes=[bstats])
                units.append(unit)
                n += 1
        return units

    def rec_stats_pe():
        for _b in (bE, bD):
            if _b.w is not None:
                bM.rs.append(_b.w)
            bM.rs.extend(_b.rs)
            _b.w, _b.rs = None, []
        p.op("vt", lambda e: e.tensor_scalar(stats[:, 0:HB], stats[:, 0:HB], -1.0, None, ALU.mult),
             reads=[bstats], writes=[bstats])
        p.op("te", lambda e: e.transpose(psb[2][0:HB, 0:128], stats[:, 0:HB], ident[:]),
             reads=[bstats], writes=[bpb[2]], extra=[bconst.w], inc=1)
        p.op("vt", lambda e: e.tensor_copy(statsT[0:HB, :], psb[2][0:HB, 0:128]),
             reads=[bpb[2]], writes=[bstatsT])
        p.dma("sy", negm_d.rearrange("o (a b) -> (o a) b", a=HB), statsT[0:HB, :],
              group="nmw", reads=[bstatsT], writes=[bnegm_d])
        rows = [(row[0:1, :], brow, "row"), (fbE[0:1, 0:512], brow2, "row2")]
        for c in range(HC):
            rv, rb, rg = rows[c % 2]
            p.dma("sy", rv, negm_d[:, c * 512:(c + 1) * 512], group=rg, extra=[bnegm_d.w], writes=[rb])
            p.op("te", lambda e, rv=rv: e.matmul(psb[3][:, :], ones1[:], rv, start=True, stop=True),
                 reads=[rb], writes=[bpb[3]], extra=[bconst.w], inc=1)
            p.op("vt", lambda e, c=c: e.tensor_copy(fbM[:, c * 512:(c + 1) * 512], psb[3][:, :]),
                 reads=[bpb[3]], writes=[bM])

    def rec_tt_dve(j0, j1):
        units = []
        n = 0
        for j in range(j0, j1):
            k = j % 2
            for hh in range(NEH):
                def load(j=j, hh=hh, n=n):
                    sl = slice(hh * EBC, (hh + 1) * EBC)
                    buf, bbuf = ebufs[n % 2]
                    p.dma("sy", buf, gfull_d[j * 128:(j + 1) * 128, sl], group="er%d" % (n % 2),
                          extra=[bgfull.w], writes=[bbuf])

                def work(j=j, k=k, hh=hh, n=n):
                    sl = slice(hh * EBC, (hh + 1) * EBC)
                    buf, bbuf = ebufs[n % 2]
                    p.op("gp", lambda e, sl=sl, buf=buf: e.tensor_tensor(buf, buf, fbM[:, sl], ALU.add),
                         reads=[bM], writes=[bbuf])
                    p.op("sc", lambda e, k=k, sl=sl, buf=buf: e.activation(tbuf[k][:, sl], buf, AF.Exp),
                         reads=[bbuf], writes=[btb[k]])
                    if hh == NEH - 1:
                        p.dma("sy", tt_d[j], tbuf[k], group="tw%d" % k, reads=[btb[k]], writes=[btt_d[j]])
                units.append((load, work))
                n += 1
        flat = []
        prev_work = None
        for (ld, wk) in units:
            if prev_work is None:
                flat.append(ld)
            else:
                flat.append(lambda ld=ld, pw=prev_work: (pw(), ld()))
            prev_work = wk
        flat.append(prev_work)
        return flat

    def rec_cw_prep(ex):
        kcw = ex % 2
        rows = [(row[0:1, :], brow, "row"), (fbE[0:1, 0:512], brow2, "row2")]
        for c in range(TCH):
            rv, rb, rg = rows[c % 2]
            p.dma("sy", rv, cwt_d[ex - 1:ex, c * 512:(c + 1) * 512], group=rg, extra=[bcwt_d.w], writes=[rb])
        for c in range(TCH):
            rv, rb, rg = rows[c % 2]
            p.op("te", lambda e, c=c, rv=rv: e.matmul(psb[2][:, :], ones1[:], rv, start=True, stop=True),
                 reads=[rb], writes=[bpb[2]], extra=[bconst.w], inc=1)
            p.op("vt", lambda e, c=c, kcw=kcw: e.tensor_copy(cwb[kcw][:, c * 512:(c + 1) * 512], psb[2][:, :]),
                 reads=[bpb[2]], writes=[bcw[kcw]])

    def rec_expert(ex, unit_hooks=None, mid_fn=None):
        is_shared = (ex == 0)
        kcw = ex % 2
        hooks = list(unit_hooks or [])
        per_strip = (len(hooks) + IB - 1) // IB if hooks else 0
        def rec_strip(i):
            kk = i % NSLOT
            p.dma("gp", wpair_view[kk][:].rearrange("p (s h) -> p s h", s=2),
                  gup_d[ex, i].rearrange("s p h -> p s h"),
                  group="wg%d" % kk, writes=[bwg[kk], bwu[kk]])
        for i in range(min(NSLOT, IB)):
            rec_strip(i)
        for i in range(IB):
            kk = i % NSLOT
            sA = 0 if (i % 2 == 0) else 4
            pg, bg_ = [psb[sA], psb[sA + 1]][:TCH], [bpb[sA], bpb[sA + 1]][:TCH]
            pu, bu_ = [psb[sA + 2], psb[sA + 3]][:TCH], [bpb[sA + 2], bpb[sA + 3]][:TCH]
            for sel, pp, bpp in ((0, pg, bg_), (1, pu, bu_)):
                wv, bb_ = ((wg_view, bwg) if sel == 0 else (wu_view, bwu))
                wv, bb_ = wv[kk], bb_[kk]
                for hb in range(HB):
                    p.op("te", lambda e, wv=wv, hb=hb: e.ldweights(wv[:, hb * 128:(hb + 1) * 128]),
                         extra=[bb_.w], inc=0)
                    for c in range(TCH):
                        first, lastm = (hb == 0), (hb == HB - 1)
                        p.op("te", lambda e, wv=wv, hb=hb, c=c, pp=pp, f=first, l=lastm: e.matmul(
                            pp[c][:], wv[:, hb * 128:(hb + 1) * 128], xt_h[hb][:, c * 512:(c + 1) * 512],
                            start=f, stop=l),
                            reads=([bb_, bxt[hb]] if lastm else []),
                            writes=([bpp[c]] if lastm else []),
                            war=([bpp[c]] if first else []),
                            extra=([bb_.w, bxt[hb].w] if not lastm else []),
                            inc=(1 if lastm else 0))
            for c in range(TCH):
                p.op("sc", lambda e, c=c, pg=pg: e.activation(loscr[:, c * 512:(c + 1) * 512], pg[c][:], AF.Silu),
                     reads=[bg_[c]], writes=[blo])
                p.op("vt", lambda e, c=c, i=i, pu=pu: e.tensor_tensor(
                    h_i[i][:, c * 512:(c + 1) * 512], loscr[:, c * 512:(c + 1) * 512], pu[c][:], ALU.mult),
                    reads=[blo, bu_[c]], writes=[bh[i]])
                if not is_shared:
                    p.op("vt", lambda e, c=c, i=i, kcw=kcw: e.tensor_tensor(
                        h_i[i][:, c * 512:(c + 1) * 512], h_i[i][:, c * 512:(c + 1) * 512],
                        cwb[kcw][:, c * 512:(c + 1) * 512], ALU.mult),
                        reads=[bcw[kcw]], writes=[bh[i]])
            if i + NSLOT < IB:
                rec_strip(i + NSLOT)
            if i == max(IB - NSLOT - 1, 0):
                dnt_toks[ex] = p.dma("gp", wbig[:, 0:IB * H].rearrange("p (i h) -> p i h", i=IB),
                                     dnt_d[ex].rearrange("i p h -> p i h"), group="dnt", writes=[bdt])
            for _ in range(per_strip):
                if hooks:
                    hooks.pop(0)()
        while hooks:
            hooks.pop(0)()
        if mid_fn is not None:
            mid_fn()
        last_down = None
        for t in range(TB):
            sA = 0 if (t % 2 == 0) else 4
            pd, bd_ = [psb[sA + c] for c in range(HC)], [bpb[sA + c] for c in range(HC)]
            for i in range(IB):
                p.op("te", lambda e, i=i, t=t: e.ldweights(h_i[i][:, t * 128:(t + 1) * 128]),
                     extra=[bh[i].w, dnt_toks[ex]], inc=0)
                for c in range(HC):
                    first, lastm = (i == 0), (i == IB - 1)
                    last_down = p.op("te", lambda e, i=i, c=c, t=t, pd=pd, f=first, l=lastm: e.matmul(
                        pd[c][:], h_i[i][:, t * 128:(t + 1) * 128], dt_i[i][:, c * 512:(c + 1) * 512],
                        start=f, stop=l),
                        reads=([bh[i], bdt] if lastm else []),
                        writes=([bd_[c]] if lastm else []),
                        war=([bd_[c]] if first else []),
                        extra=([bh[i].w, dnt_toks[ex]] if not lastm else []),
                        inc=(1 if lastm else 0))
            for c in range(HC):
                p.op("vt", lambda e, c=c, t=t, pd=pd: e.tensor_tensor(
                    acc_t[t][:, c * 512:(c + 1) * 512], acc_t[t][:, c * 512:(c + 1) * 512],
                    pd[c][:], ALU.add),
                    reads=[bd_[c]], writes=[bacc[t]])
        for i in range(IB):
            bh[i].rs.append(last_down)
        bdt.rs.append(last_down)

    stats_units = []
    tta_units = []
    ttb_units = []
    for ex in range(NE):
        hooks = None
        if ex == 3:
            stats_units = rec_stats_dve()
            hooks = stats_units[:len(stats_units) // 2]
        elif ex == 4:
            hooks = stats_units[len(stats_units) // 2:]
        elif ex == 6:
            tta_units = rec_tt_dve(0, HB // 2)
            hooks = tta_units[:len(tta_units) // 2]
        elif ex == 7:
            hooks = tta_units[len(tta_units) // 2:]
        elif ex == 8:
            ttb_units = rec_tt_dve(HB // 2, HB)
            hooks = ttb_units[:len(ttb_units) // 2]
        elif ex == 9:
            hooks = ttb_units[len(ttb_units) // 2:]
        mids = []
        if ex == 0:
            mids.append(rec_cwt)
            mids.append(lambda: rec_cw_prep(1))
        if 1 <= ex < NE - 1:
            mids.append(lambda ex=ex: rec_cw_prep(ex + 1))
        if ex == 5:
            mids.append(rec_stats_pe)
        mid = (lambda mids=mids: [m() for m in mids]) if mids else None
        rec_expert(ex, unit_hooks=hooks, mid_fn=mid)
        if ex == min(1, NE - 2):
            cc_tok = p.op("gp", lambda e: e.collective_compute(
                "AllReduce", ALU.add, replica_groups=[list(range(cfg.n_cores))],
                ins=[gpart_d], outs=[gfull_d]),
                extra=[bgpart.w], sem="cc", inc=1)
            bgfull.w = cc_tok

    PW = min(2, HC)
    bttrA = Buf("ttrA")
    bttrA.w = bdt.w
    bttrA.rs = list(bdt.rs)
    ttr2 = [wbig[:, j * (PW * 512):(j + 1) * (PW * 512)] for j in range(HB)]
    bttrB = [Buf(f"ttrB{q}") for q in range(4)]
    for q in range(4):
        bttrB[q].w = bdt.w
        bttrB[q].rs = list(bdt.rs)
    QS = max(HB // 4, 1)
    for pr in range(HC // PW):
        for j in range(HB):
            q = j // QS if QS else 0
            tok = p.dma("sc", ttr2[j], tt_d[j, :, pr * PW * 512:(pr + 1) * PW * 512],
                        group="ta%d" % min(q, 3), extra=[btt_d[HB - 1].w, btt_d[HB - 2].w],
                        writes=[bttrB[min(q, 3)]])
            if (j + 1) % QS == 0 or j == HB - 1:
                bttrA.w = tok
        for q in range(4):
            bttrB[min(q, 3)].w = bttrB[min(q, 3)].w
        p.op("te", lambda e: e.ldweights(ones128[:]), extra=[bconst.w], inc=0)
        for j in range(HB):
            q = min(j // QS, 3)
            first, lastm = (j == 0), (j == HB - 1)
            for c in range(PW):
                p.op("te", lambda e, j=j, c=c, f=first, l=lastm: e.matmul(
                    psb[2 + c][:], ones128[:], ttr2[j][:, c * 512:(c + 1) * 512], start=f, stop=l),
                    reads=([bttrB[q]] if lastm else []),
                    writes=([bpb[2 + c]] if lastm else []),
                    war=([bpb[2 + c]] if first else []),
                    extra=([bttrB[q].w, bconst.w] if not lastm else [bconst.w]),
                    inc=(1 if lastm else 0))
        for c in range(PW):
            ic = pr * PW + c
            p.op("vt", lambda e, ic=ic, c=c: e.reciprocal(fbM[:, ic * 512:(ic + 1) * 512], psb[2 + c][:]),
                 reads=[bpb[2 + c]], writes=[bM])
        for t in range(TB):
            s0 = 0 if t % 2 == 0 else 4
            banks = [psb[s0 + c] for c in range(PW)]
            bbs = [bpb[s0 + c] for c in range(PW)]
            for j in range(HB):
                q = min(j // QS, 3)
                p.op("te", lambda e, j=j, t=t: e.ldweights(xt_h[j][:, t * 128:(t + 1) * 128]),
                     extra=[bxt[j].w], inc=0)
                first, lastm = (j == 0), (j == HB - 1)
                for c in range(PW):
                    p.op("te", lambda e, j=j, t=t, c=c, bank=banks[c], f=first, l=lastm: e.matmul(
                        bank[:], xt_h[j][:, t * 128:(t + 1) * 128], ttr2[j][:, c * 512:(c + 1) * 512],
                        start=f, stop=l),
                        reads=([bttrB[q], bxt[j]] if lastm else []),
                        writes=([bbs[c]] if lastm else []),
                        war=([bbs[c]] if first else []),
                        extra=([bttrB[q].w, bxt[j].w] if not lastm else []),
                        inc=(1 if lastm else 0))
            for c in range(PW):
                ic = pr * PW + c
                p.op("vt", lambda e, ic=ic, bank=banks[c]: e.tensor_tensor(
                    fbE[:, 0:512], bank[:], fbM[:, ic * 512:(ic + 1) * 512], ALU.mult),
                    reads=[bbs[c], bM], writes=[bE])
                p.op("vt", lambda e, t=t, ic=ic: e.tensor_tensor(
                    acc_t[t][:, ic * 512:(ic + 1) * 512], acc_t[t][:, ic * 512:(ic + 1) * 512],
                    fbE[:, 0:512], ALU.add),
                    reads=[bE], writes=[bacc[t]])
        pair_end = bbs[PW - 1].w
        for q in range(4):
            if pair_end is not None:
                bttrB[q].rs.append(pair_end)

    out_tok = None
    for t in range(TB):
        out_tok = p.dma("sc", out_d[t * 128:(t + 1) * 128, :], acc_t[t], group="out", reads=[bacc[t]])
    p.wait("sc", out_tok)
    p.wait("sy", out_tok)

    block = es.enter_context(nc.Block())
    p.emit(block, es)
    es.close()
    return nc


def _prep_inputs(cfg, x, gate_w, ebias_np, gate_p, up_p, down_p, sh_g, sh_u, sh_d):
    H, I, TC, E, NC = cfg.H, cfg.I, cfg.TC, cfg.E, cfg.n_cores
    IB, HB = cfg.IB, cfg.HB
    x = np.ascontiguousarray(np.asarray(x, dtype=np.float32).reshape(-1, H))
    NE = E + 1

    def tile_gu(g, u):
        def t(a):
            return np.asarray(a, np.float32).reshape(IB, 128, HB, 128).transpose(0, 3, 2, 1).reshape(IB, 128, HB * 128)
        return np.stack([t(g), t(u)], axis=1)

    gup = np.empty((NE, IB, 2, 128, HB * 128), np.float32)
    gup[0] = tile_gu(sh_g, sh_u)
    for e in range(E):
        gup[e + 1] = tile_gu(gate_p[e], up_p[e])

    def tile_dn(d):
        return np.asarray(d, np.float32).T.reshape(IB, 128, H)

    dnt = np.empty((NE, IB, 128, H), np.float32)
    dnt[0] = tile_dn(sh_d)
    for e in range(E):
        dnt[e + 1] = tile_dn(down_p[e])

    wt = np.ascontiguousarray(np.asarray(gate_w, np.float32).T.reshape(HB, 128, E))
    eb = np.asarray(ebias_np, np.float32).reshape(1, E)
    ones1 = np.ones((1, 128), np.float32)
    ones128 = np.ones((128, 128), ml_dtypes.bfloat16)
    ident = np.eye(128, dtype=np.float32)

    shards = np.split(x, NC, axis=0)
    return [{"x": np.ascontiguousarray(shards[c]), "gup": gup, "dnt": dnt, "wt": wt, "eb": eb,
             "ones1": ones1, "ones128": ones128, "ident": ident} for c in range(NC)]


def kernel(L0_attn_x, L0_ffn_gate_weight, L0_ffn_expert_bias, L0_ffn_gate_proj,
           L0_ffn_up_proj, L0_ffn_down_proj, L0_ffn_shared_gate, L0_ffn_shared_up,
           L0_ffn_shared_down):
    cfg = CFG
    in_maps = _prep_inputs(cfg, L0_attn_x, L0_ffn_gate_weight, L0_ffn_expert_bias,
                           L0_ffn_gate_proj, L0_ffn_up_proj, L0_ffn_down_proj,
                           L0_ffn_shared_gate, L0_ffn_shared_up, L0_ffn_shared_down)
    nc = build_nc(cfg)
    res = run_bass_kernel_spmd(nc, in_maps, core_ids=list(range(cfg.n_cores)))
    out = np.concatenate([r["out"] for r in res.results], axis=0)
    return out.reshape(np.asarray(L0_attn_x).shape)


# version 28
# speedup vs baseline: 1.0168x; 1.0168x over previous
import numpy as np
import ml_dtypes
from contextlib import ExitStack

import concourse.bass as bass
import concourse.mybir as mybir
from concourse.bass_utils import run_bass_kernel_spmd

F32 = mybir.dt.float32
BF16 = mybir.dt.bfloat16
AF = mybir.ActivationFunctionType
ALU = mybir.AluOpType
AX = mybir.AxisListType


class Cfg:
    def __init__(self, H=2048, I=1024, TC=1024, E=16, n_cores=8,
                 n_group=4, topk=4, topk_group=2, scaling=2.5):
        self.H, self.I, self.TC, self.E, self.n_cores = H, I, TC, E, n_cores
        self.n_group, self.topk, self.topk_group, self.scaling = n_group, topk, topk_group, scaling
        self.HB = H // 128
        self.IB = I // 128
        self.TB = TC // 128
        self.HC = H // 512
        self.TCH = TC // 512
        self.gsz = E // n_group
        self.EBC = min(H, 1024)
        self.NEH = H // self.EBC
        assert H % 512 == 0 and I % 128 == 0 and TC % 512 == 0 and self.HC <= 4


CFG = Cfg()

_ENG2BLK = {"te": "tensor", "vt": "vector", "sc": "scalar", "gp": "gpsimd", "sy": "sync"}


class Buf:
    __slots__ = ("w", "rs", "name")

    def __init__(self, name=""):
        self.w = None
        self.rs = []
        self.name = name

    def deps(self):
        return ([self.w] if self.w is not None else []) + list(self.rs)


class Prog:
    def __init__(self, nc):
        self.nc = nc
        self.ops = {e: [] for e in _ENG2BLK}
        self.cnt = {}
        self.sems = {}

    def _tok(self, sem, inc):
        v = self.cnt.get(sem, 0) + inc
        self.cnt[sem] = v
        return (sem, v)

    def op(self, eng, emit, reads=(), writes=(), extra=(), war=(), inc=1, sem=None, skip_sem=None):
        sem = sem or eng
        deps = [t for t in extra if t is not None]
        for b in reads:
            if b.w is not None:
                deps.append(b.w)
        for b in writes:
            deps.extend(b.deps())
        for b in war:
            deps.extend(b.deps())
        if skip_sem is not None:
            deps = [t for t in deps if t[0] != skip_sem]
        tok = self._tok(sem, inc) if inc else None
        self.ops[eng].append((tuple(deps), emit, sem if inc else None, inc))
        if tok is not None:
            for b in reads:
                b.rs.append(tok)
            for b in writes:
                b.w = tok
                b.rs = []
        return tok

    def dma(self, eng, out, in_, group=None, reads=(), writes=(), extra=(), **kw):
        g = "d_" + (group or eng)
        return self.op(eng,
                       (lambda e, out=out, in_=in_, kw=kw: e.dma_start(out=out, in_=in_, **kw)),
                       reads=reads, writes=writes, extra=extra, inc=16, sem=g, skip_sem=g)

    def wait(self, eng, tok):
        self.ops[eng].append(((tok,), None, None, 0))

    def emit(self, block, es):
        for name in self.cnt:
            self.sems[name] = es.enter_context(self.nc.semaphore(name))
        for eng, blkname in _ENG2BLK.items():
            ops = self.ops[eng]
            if not ops:
                continue

            def body(e, ops=ops):
                hw = {}
                for deps, emitfn, sem, inc in ops:
                    for (s, v) in deps:
                        if hw.get(s, 0) < v:
                            e.wait_ge(self.sems[s], v)
                            hw[s] = v
                    if emitfn is None:
                        continue
                    ins = emitfn(e)
                    if inc:
                        ins.then_inc(self.sems[sem], inc)
            getattr(block, blkname)(body)


def build_nc(cfg):
    H, I, TC, E = cfg.H, cfg.I, cfg.TC, cfg.E
    HB, IB, TB, HC, TCH = cfg.HB, cfg.IB, cfg.TB, cfg.HC, cfg.TCH
    NG, TK, TKG, gsz = cfg.n_group, cfg.topk, cfg.topk_group, cfg.gsz
    EBC, NEH = cfg.EBC, cfg.NEH
    NE = E + 1

    nc = bass.Bass("TRN2", target_bir_lowering=False, debug=False)

    x_d = nc.dram_tensor("x", [TC, H], F32, kind="ExternalInput").ap()
    gup_d = nc.dram_tensor("gup", [NE, IB, 2, 128, HB * 128], F32, kind="ExternalInput").ap()
    dnt_d = nc.dram_tensor("dnt", [NE, IB, 128, H], F32, kind="ExternalInput").ap()
    wt_d = nc.dram_tensor("wt", [HB, 128, E], F32, kind="ExternalInput").ap()
    eb_d = nc.dram_tensor("eb", [1, E], F32, kind="ExternalInput").ap()
    ones1_d = nc.dram_tensor("ones1", [1, 128], F32, kind="ExternalInput").ap()
    ones128_d = nc.dram_tensor("ones128", [128, 128], BF16, kind="ExternalInput").ap()
    ident_d = nc.dram_tensor("ident", [128, 128], F32, kind="ExternalInput").ap()
    out_d = nc.dram_tensor("out", [TC, H], F32, kind="ExternalOutput").ap()
    xn3_d = nc.dram_tensor("xn3", [HB, TC, 128], BF16, kind="Internal").ap()
    lo3_d = nc.dram_tensor("lo3", [HB, TC, 128], BF16, kind="Internal").ap()
    gpart_d = nc.dram_tensor("gpart", [H, H], BF16, kind="Internal").ap()
    gfull_shared = cfg.n_cores > 4
    gfull_d = nc.dram_tensor("gfull", [H, H], BF16, kind="Internal",
                             addr_space=("Shared" if gfull_shared else "Local")).ap()
    tt_d = nc.dram_tensor("ttd", [HB, 128, H], BF16, kind="Internal").ap()
    cwt_d = nc.dram_tensor("cwtd", [E, TC], F32, kind="Internal").ap()
    negm_d = nc.dram_tensor("negmd", [1, H], F32, kind="Internal").ap()

    es = ExitStack()
    sb = lambda n, s, d: es.enter_context(nc.sbuf_tensor(n, s, d))

    acc = sb("acc", [128, TB * H], F32)
    xt = sb("xt", [128, HB * TC], BF16)
    xnb = sb("xnb", [128, max(TB * H, IB * TC + 2 * H + 2 * TC)], BF16)
    wbig = sb("wbig", [128, max(IB * H, 2 * HB * 512)], BF16)
    wgu = sb("wgu", [128, 6 * H], BF16)
    fbE = sb("fbE", [128, 512], F32)
    fbM = sb("fbM", [128, H], F32)
    loscr = sb("loscr", [128, max(EBC, TCH * 512)], BF16)
    ebb = sb("ebb", [128, 2 * EBC], BF16)
    lgt = sb("lgt", [128, 128], F32)
    r1 = sb("r1", [128, 128], F32)
    r2 = sb("r2", [128, 128], F32)
    r3 = sb("r3", [128, 128], F32)
    r4 = sb("r4", [128, 128], F32)
    gsb = sb("gsb", [128, 96], F32)
    statsT = r1[0:16, 0:128]
    stats = sb("stats", [128, 16], F32)
    w16 = sb("w16", [128, 2 * HB * E], BF16)
    ident = sb("ident_sb", [128, 128], F32)
    ones1 = sb("ones1_sb", [1, 128], F32)
    ones128 = sb("ones128_sb", [128, 128], BF16)
    ebias = sb("ebias_sb", [1, E], F32)
    bias_bc = sb("bias_bc", [128, E], F32)
    row = sb("row", [1, 512], F32)
    ln = sb("ln", [128, 4 * TB + 1], F32)
    bnst = sb("bnst", [128, 6 * 4], F32)
    gpad = bnst[:, 0:8]
    m8 = bnst[:, 8:16]

    psb = [es.enter_context(nc.psum_tensor(f"pb{i}", [128, 512], F32)) for i in range(8)]

    acc_t = [acc[:, t * H:(t + 1) * H] for t in range(TB)]
    xt_h = [xt[:, h * TC:(h + 1) * TC] for h in range(HB)]
    xn_t = [xnb[:, t * H:(t + 1) * H] for t in range(TB)]
    h_i = [xnb[:, i * TC:(i + 1) * TC] for i in range(IB)]
    tbuf = [xnb[:, IB * TC + k * H: IB * TC + (k + 1) * H] for k in range(2)]
    cwb = [xnb[:, IB * TC + 2 * H + k * TC: IB * TC + 2 * H + (k + 1) * TC] for k in range(2)]
    lostr = cwb
    dt_i = [wbig[:, i * H:(i + 1) * H] for i in range(IB)]
    ttr_w = HB * 512
    ttr = [wbig[:, k * ttr_w:(k + 1) * ttr_w] for k in range(2)]
    whi = [w16[:, hb * E:(hb + 1) * E] for hb in range(HB)]
    wlo = [w16[:, (HB + hb) * E:(HB + hb + 1) * E] for hb in range(HB)]

    p = Prog(nc)

    bacc = [Buf(f"acc{t}") for t in range(TB)]
    bxt = [Buf(f"xt{h}") for h in range(HB)]
    bxn = [Buf(f"xn{t}") for t in range(TB)]
    bh = [Buf(f"h{i}") for i in range(IB)]
    btb = [Buf(f"tb{k}") for k in range(2)]
    bcw = [Buf(f"cw{k}") for k in range(2)]
    bdt = Buf("dt")
    bttr = [Buf(f"ttr{k}") for k in range(2)]
    bwg = [Buf(f"wg{k}") for k in range(3)]
    bwu = [Buf(f"wu{k}") for k in range(3)]
    bE = Buf("fbE")
    bebb = [Buf("ebb0"), Buf("ebb1")]
    bM = Buf("fbM")
    bD = Buf("fbD")
    blo = Buf("loscr")
    bpb = [Buf(f"pb{i}") for i in range(8)]
    blgt, br1, br2, br3, br4 = Buf("lgt"), Buf("r1"), Buf("r2"), Buf("r3"), Buf("r4")
    bgs = Buf("gsb")
    bgpad = Buf("gpad")
    bm8 = Buf("m8")
    bstats, bstatsT = Buf("stats"), Buf("statsT")
    bw16 = Buf("w16")
    bconst, bebias, bbias = Buf("const"), Buf("ebias"), Buf("bias_bc")
    brow = Buf("row")
    brow2 = Buf("row2")
    bln = Buf("ln")
    bbn = Buf("bnst")
    bln_t = [Buf(f"ln{t}") for t in range(TB)]
    bxn3, blo3 = Buf("xn3"), Buf("lo3")
    bgpart, bgfull = Buf("gpart"), Buf("gfull")
    btt_d = [Buf(f"ttd{j}") for j in range(HB)]
    bcwt_d = Buf("cwtd")
    bnegm_d = Buf("negmd")

    NSLOT = 3
    wg_view = [wgu[:, (2 * k) * H:(2 * k + 1) * H] for k in range(NSLOT)]
    wu_view = [wgu[:, (2 * k + 1) * H:(2 * k + 2) * H] for k in range(NSLOT)]
    wpair_view = [wgu[:, (2 * k) * H:(2 * k + 2) * H] for k in range(NSLOT)]

    for t in range(TB):
        p.dma("sy", acc_t[t], x_d[t * 128:(t + 1) * 128, :], group="ac%d" % (0 if t < TB // 2 else 1),
              writes=[bacc[t]])
    p.dma("sy", ident[:], ident_d, group="cst", writes=[bconst])
    p.dma("sy", ones1[:], ones1_d, group="cst")
    p.dma("sy", ones128[:], ones128_d, group="cst")
    tok_c = p.dma("sy", ebias[:], eb_d, group="cst", writes=[bebias])
    bconst.w = tok_c
    bebias.w = tok_c
    for t in range(TB):
        bacc[t].w = bacc[(TB // 2 - 1) if t < TB // 2 else (TB - 1)].w
    p.dma("gp", w16[:, 0:HB * E].rearrange("p (b e) -> p b e", b=HB),
          wt_d.rearrange("b p e -> p b e"), group="w16", writes=[bw16])
    p.op("vt", lambda e: e.memset(ln[:, 4 * TB:4 * TB + 1], 1e-5), writes=[bln])
    for hb in range(HB):
        p.dma("sy", stats[:, 0:E], wt_d[hb], group="wst", writes=[bstats])
        p.op("vt", lambda e, hb=hb: e.tensor_tensor(wlo[hb], stats[:, 0:E], whi[hb], ALU.subtract),
             reads=[bstats, bw16], writes=[bw16])

    NH = H // EBC
    fbD_v = fbM[:, 0:EBC]
    fbD_hi = fbM[:, EBC:2 * EBC] if NH > 1 else fbM[:, 0:EBC]

    def rec_lo(t):
        c0 = 4 * t
        nmr, rstd = ln[:, c0 + 3:c0 + 4], ln[:, c0 + 2:c0 + 3]
        for hh in range(NH):
            sl = slice(hh * EBC, (hh + 1) * EBC)
            buf, bbuf = (fbD_hi, bE) if hh % 2 == 0 else (fbD_v, bD)
            p.op("gp", lambda e, t=t, sl=sl, buf=buf, o=nmr, r=rstd:
                 e.tensor_scalar(buf, acc_t[t][:, sl], o, r, ALU.add, ALU.mult),
                 reads=[bacc[t], bln_t[t]], writes=[bbuf])
            p.op("gp", lambda e, t=t, sl=sl, buf=buf: e.tensor_tensor(loscr[:, 0:EBC], buf, xn_t[t][:, sl],
                                                                      ALU.subtract),
                 reads=[bbuf, bxn[t]], writes=[blo])
            hb0 = hh * (EBC // 128)
            nhb = EBC // 128
            p.dma("gp", lo3_d[hb0:hb0 + nhb, t * 128:(t + 1) * 128, :].rearrange("b t h -> t b h"),
                  loscr[:, 0:EBC].rearrange("t (b h) -> t b h", b=nhb), group="lo3", reads=[blo], writes=[blo3])

    WAVE = 4
    for w0 in range(0, TB, WAVE):
        wave = list(range(w0, min(w0 + WAVE, TB)))
        for t in wave:
            c0 = 4 * t
            for c4 in range(HC):
                p.op("vt", lambda e, t=t, c4=c4: e.bn_stats(bnst[:, 6 * c4:6 * (c4 + 1)],
                                                            acc_t[t][:, 512 * c4:512 * (c4 + 1)]),
                     reads=[bacc[t]], writes=[bbn])
            p.op("vt", lambda e, t=t, c0=c0: e.bn_aggr(ln[:, c0:c0 + 2], bnst[:, 0:6 * HC]),
                 reads=[bbn], writes=[bln_t[t]])
        nw = len(wave)
        vsl = ln[:, 4 * w0 + 1: 4 * (w0 + nw - 1) + 2: 4]
        rsl = ln[:, 4 * w0 + 2: 4 * (w0 + nw - 1) + 3: 4]
        msl = ln[:, 4 * w0: 4 * (w0 + nw - 1) + 1: 4]
        nsl = ln[:, 4 * w0 + 3: 4 * (w0 + nw - 1) + 4: 4]
        wbufs = [bln_t[t] for t in wave]
        p.op("sc", lambda e, vsl=vsl, rsl=rsl: e.activation(rsl, vsl, AF.Sqrt, bias=ln[:, 4 * TB:4 * TB + 1]),
             reads=wbufs, extra=[bln.w], writes=wbufs)
        p.op("vt", lambda e, rsl=rsl: e.reciprocal(rsl, rsl), reads=wbufs, writes=wbufs)
        p.op("vt", lambda e, msl=msl, nsl=nsl: e.tensor_scalar(nsl, msl, -1.0, None, ALU.mult),
             reads=wbufs, writes=wbufs)
        for t in wave:
            c0 = 4 * t
            nmr, rstd = ln[:, c0 + 3:c0 + 4], ln[:, c0 + 2:c0 + 3]
            for hh in range(NH):
                sl = slice(hh * EBC, (hh + 1) * EBC)
                eng = "vt" if hh % 2 == 0 else "gp"
                p.op(eng, lambda e, t=t, sl=sl, o=nmr, r=rstd:
                     e.tensor_scalar(xn_t[t][:, sl], acc_t[t][:, sl], o, r, ALU.add, ALU.mult),
                     reads=[bacc[t], bln_t[t]], writes=[bxn[t]])
            p.dma("sy", xn3_d[:, t * 128:(t + 1) * 128, :].rearrange("b t h -> t b h"),
                  xn_t[t].rearrange("t (b h) -> t b h", b=HB), group="xn3", reads=[bxn[t]], writes=[bxn3])
    for t in range(TB):
        rec_lo(t)

    for hb in range(HB):
        p.dma("sc", xt_h[hb], xn3_d[hb], group="trx", extra=[bxn3.w], writes=[bxt[hb]], transpose=True)
    for hb in range(HB):
        bxt[hb].w = bxt[HB - 1].w


    def rec_router():
        lps, blps = [psb[0], psb[1]][:TCH], [bpb[0], bpb[1]][:TCH]
        nrm = [0]
        total_rmm = 3 * HB

        def rmm(lw, lw_bufs, rhs, rhs_bufs):
            k = nrm[0]
            nrm[0] += 1
            p.op("te", lambda e, lw=lw: e.ldweights(lw), extra=[b.w for b in lw_bufs], inc=0)
            for c in range(TCH):
                first, lastm = (k == 0), (k == total_rmm - 1)
                p.op("te", lambda e, c=c, lw=lw, rhs=rhs, f=first, l=lastm: e.matmul(
                    lps[c][0:E, :], lw, rhs[:, c * 512:(c + 1) * 512], start=f, stop=l),
                    reads=list(rhs_bufs) + list(lw_bufs),
                    writes=([blps[c]] if (first or lastm) else []), inc=1)

        for hb in range(min(2, HB)):
            p.dma("sc", lostr[hb % 2], lo3_d[hb], group="tl%d" % (hb % 2), extra=[blo3.w],
                  writes=[bcw[hb % 2]], transpose=True)
        for hb in range(HB):
            rmm(whi[hb], [bw16], xt_h[hb], [bxt[hb]])
            rmm(wlo[hb], [bw16], xt_h[hb], [bxt[hb]])
        for hb in range(HB):
            rmm(whi[hb], [bw16], lostr[hb % 2], [bcw[hb % 2]])
            if hb + 2 < HB:
                p.dma("sc", lostr[(hb + 2) % 2], lo3_d[hb + 2], group="tl%d" % ((hb + 2) % 2),
                      extra=[blo3.w], writes=[bcw[(hb + 2) % 2]], transpose=True)
        for tb in range(TB):
            c, off = (tb * 128) // 512, (tb * 128) % 512
            p.op("vt", lambda e, c=c, off=off: e.tensor_copy(statsT, lps[c][0:E, off:off + 128]),
                 reads=[blps[c]], writes=[bstatsT])
            p.op("te", lambda e: e.transpose(psb[2][0:128, 0:E], statsT, ident[0:E, 0:E]),
                 reads=[bstatsT], extra=[bconst.w], writes=[bpb[2]], inc=1)
            p.op("vt", lambda e, tb=tb: e.tensor_copy(lgt[:, tb * E:(tb + 1) * E], psb[2][0:128, 0:E]),
                 reads=[bpb[2]], writes=[blgt])
        p.op("te", lambda e: e.matmul(psb[3][:, 0:E], ones1[:], ebias[:], start=True, stop=True),
             reads=[bebias], writes=[bpb[3]], extra=[bconst.w], inc=1)
        p.op("vt", lambda e: e.tensor_copy(bias_bc[:], psb[3][:, 0:E]), reads=[bpb[3]], writes=[bbias])

        for _b in (bgpad, bm8):
            _b.w = bbn.w
            _b.rs = list(bbn.rs)
        p.op("vt", lambda e: e.memset(gpad[:, 4:8], -1e30), writes=[bgpad])
        br1.rs.extend(bstatsT.rs)
        if bstatsT.w is not None:
            br1.rs.append(bstatsT.w)
        NBE = TB * E
        p.op("sc", lambda e: e.activation(r1[:, 0:NBE], lgt[:, 0:NBE], AF.Sigmoid), reads=[blgt], writes=[br1])
        for tb in range(TB):
            p.op("vt", lambda e, tb=tb: e.tensor_tensor(r2[:, tb * E:(tb + 1) * E], r1[:, tb * E:(tb + 1) * E],
                                                        bias_bc[:], ALU.add),
                 reads=[br1, bbias], writes=[br2])
        NBG = TB * NG
        b4 = r2[:, 0:NBE].rearrange("p (q s) -> p q s", s=gsz)
        gv = [gsb[:, k * 32:k * 32 + NBG] for k in range(3)]
        pairs = [(a, b) for a in range(gsz) for b in range(a + 1, gsz)]
        for n, (a, b) in enumerate(pairs):
            dst = gv[0] if n == 0 else gv[1]
            p.op("vt", lambda e, a=a, b=b, dst=dst: e.tensor_tensor(dst, b4[:, :, a], b4[:, :, b], ALU.add),
                 reads=[br2], writes=[bgs])
            if n > 0:
                p.op("vt", lambda e: e.tensor_tensor(gv[0], gv[0], gv[1], ALU.max), reads=[bgs], writes=[bgs])
        for tb in range(TB):
            sl = slice(tb * NG, (tb + 1) * NG)
            p.op("vt", lambda e, sl=sl: e.tensor_copy(gpad[:, 0:NG], gv[0][:, sl]),
                 reads=[bgs], writes=[bgpad])
            p.op("vt", lambda e: e.max(out=m8, in_=gpad), reads=[bgpad], writes=[bm8])
            p.op("vt", lambda e: e.memset(m8[:, TKG:8], -1e30), writes=[bm8])
            p.op("vt", lambda e: e.match_replace(gpad, m8, gpad, -1e30),
                 reads=[bm8], writes=[bgpad])
            p.op("vt", lambda e, sl=sl: e.tensor_copy(gv[1][:, sl], gpad[:, 0:NG]),
                 reads=[bgpad], writes=[bgs])
        p.op("vt", lambda e: e.tensor_tensor(gv[2], gv[0], gv[1], ALU.not_equal), reads=[bgs], writes=[bgs])
        for q in range(NBG):
            p.op("vt", lambda e, q=q: e.tensor_scalar(r3[:, q * gsz:(q + 1) * gsz], ones128[0:128, 0:gsz],
                                                      gv[2][:, q:q + 1], None, ALU.mult),
                 reads=[bgs], extra=[bconst.w], writes=[br3])
        p.op("vt", lambda e: e.tensor_scalar(lgt[:, 0:NBE], r3[:, 0:NBE], 1e30, -1e30, ALU.mult, ALU.add),
             reads=[br3], writes=[blgt])
        p.op("vt", lambda e: e.tensor_tensor(r4[:, 0:NBE], r2[:, 0:NBE], r3[:, 0:NBE], ALU.mult),
             reads=[br2, br3], writes=[br4])
        p.op("vt", lambda e: e.tensor_tensor(r4[:, 0:NBE], r4[:, 0:NBE], lgt[:, 0:NBE], ALU.add),
             reads=[blgt], writes=[br4])
        p.op("vt", lambda e: e.tensor_copy(lgt[:, 0:NBE], r4[:, 0:NBE]), reads=[br4], writes=[blgt])
        for tb in range(TB):
            sl = slice(tb * E, (tb + 1) * E)
            p.op("vt", lambda e, sl=sl: e.max(out=m8, in_=lgt[:, sl]), reads=[blgt], writes=[bm8])
            p.op("vt", lambda e: e.memset(m8[:, TK:8], -2e30), writes=[bm8])
            p.op("vt", lambda e, sl=sl: e.match_replace(lgt[:, sl], m8, lgt[:, sl], -2e30),
                 reads=[bm8], writes=[blgt])
        p.op("vt", lambda e: e.tensor_tensor(r3[:, 0:NBE], r4[:, 0:NBE], lgt[:, 0:NBE], ALU.not_equal),
             reads=[br4, blgt], writes=[br3])
        p.op("vt", lambda e: e.tensor_tensor(r4[:, 0:NBE], r1[:, 0:NBE], r3[:, 0:NBE], ALU.mult),
             reads=[br1, br3], writes=[br4])
        p.op("vt", lambda e: e.reduce_sum(gv[0][:, 0:TB], r4[:, 0:NBE].rearrange("p (b s) -> p b s", s=E),
                                          axis=AX.X), reads=[br4], writes=[bgs])
        p.op("vt", lambda e: e.tensor_scalar(gv[0][:, 0:TB], gv[0][:, 0:TB], 1e-20, None, ALU.add),
             reads=[bgs], writes=[bgs])
        p.op("vt", lambda e: e.reciprocal(gv[0][:, 0:TB], gv[0][:, 0:TB]), reads=[bgs], writes=[bgs])
        for tb in range(TB):
            p.op("vt", lambda e, tb=tb: e.tensor_scalar(r4[:, tb * E:(tb + 1) * E], r4[:, tb * E:(tb + 1) * E],
                                                        gv[0][:, tb:tb + 1], cfg.scaling, ALU.mult, ALU.mult),
                 reads=[bgs], writes=[br4])

    last_gram = None
    for it in range(HB):
        s0 = (it % 2) * 4
        banks = [psb[s0 + c] for c in range(HC)]
        bbs = [bpb[s0 + c] for c in range(HC)]
        for t in range(TB):
            p.op("te", lambda e, t=t, it=it: e.ldweights(xn_t[t][:, it * 128:(it + 1) * 128]),
                 extra=[bxn[t].w], inc=0)
            first, lastm = (t == 0), (t == TB - 1)
            for c in range(HC):
                last_gram = p.op("te", lambda e, t=t, c=c, it=it, bank=banks[c], f=first, l=lastm:
                                 e.matmul(bank[:], xn_t[t][:, it * 128:(it + 1) * 128],
                                          xn_t[t][:, c * 512:(c + 1) * 512], start=f, stop=l),
                                 reads=([bxn[t]] if lastm else []),
                                 writes=([bbs[c]] if lastm else []),
                                 war=([bbs[c]] if first else []),
                                 extra=([bxn[t].w] if not lastm else []),
                                 inc=(1 if lastm else 0))
        for c in range(HC):
            p.op("vt", lambda e, c=c, bank=banks[c]: e.tensor_copy(ebb[:, c * 512:(c + 1) * 512], bank[:]),
                 reads=[bbs[c]], writes=[bebb[c // 2]])
        p.dma("sy", gpart_d[it * 128:(it + 1) * 128, 0:min(H, 2048)], ebb[:, 0:min(H, 2048)],
              group="gr0", reads=[bebb[0], bebb[1]], writes=[bgpart])
    for t in range(TB):
        bxn[t].rs.append(last_gram)
    for k in range(2):
        bcw[k].rs.append(last_gram)
        btb[k].rs.append(last_gram)
    for i in range(IB):
        bh[i].rs.append(last_gram)
    rec_router()

    def rec_cwt():
        bstatsT.w = br1.w
        bstatsT.rs = list(br1.rs)
        for tb in range(TB):
            p.op("te", lambda e, tb=tb: e.transpose(psb[2][0:E, 0:128], r4[:, tb * E:(tb + 1) * E], ident[:]),
                 reads=[br4], writes=[bpb[2]], extra=[bconst.w], inc=1)
            p.op("vt", lambda e: e.tensor_copy(statsT, psb[2][0:E, 0:128]),
                 reads=[bpb[2]], writes=[bstatsT])
            p.dma("sy", cwt_d[:, tb * 128:(tb + 1) * 128], statsT, group="cwt",
                  reads=[bstatsT], writes=[bcwt_d])
    rt = bcw[0].rs + bcw[1].rs + ([bcw[0].w] if bcw[0].w else []) + ([bcw[1].w] if bcw[1].w else [])
    for k in range(2):
        bcw[k].w = None
        bcw[k].rs = list(rt)

    dnt_toks = {}

    ebufs = [(ebb[:, 0:EBC], bebb[0]), (ebb[:, EBC:2 * EBC], bebb[1])]

    def rec_stats_dve():
        units = []
        n = 0
        for it in range(HB):
            for hh in range(NEH):
                def unit(it=it, hh=hh, n=n):
                    buf, bbuf = ebufs[n % 2]
                    p.dma("sy", buf, gfull_d[it * 128:(it + 1) * 128, hh * EBC:(hh + 1) * EBC],
                          group="er%d" % (n % 2), extra=[bgfull.w], writes=[bbuf])
                    if hh == 0:
                        p.op("vt", lambda e, it=it, buf=buf: e.reduce_max(stats[:, it:it + 1], buf, axis=AX.X),
                             reads=[bbuf], writes=[bstats])
                    else:
                        p.op("vt", lambda e, buf=buf: e.reduce_max(m8[:, 0:1], buf, axis=AX.X),
                             reads=[bbuf], writes=[bm8])
                        p.op("vt", lambda e, it=it: e.tensor_tensor(stats[:, it:it + 1], stats[:, it:it + 1],
                                                                    m8[:, 0:1], ALU.max),
                             reads=[bm8], writes=[bstats])
                units.append(unit)
                n += 1
        return units

    def rec_stats_pe():
        for _b in (bE, bD):
            if _b.w is not None:
                bM.rs.append(_b.w)
            bM.rs.extend(_b.rs)
            _b.w, _b.rs = None, []
        p.op("vt", lambda e: e.tensor_scalar(stats[:, 0:HB], stats[:, 0:HB], -1.0, None, ALU.mult),
             reads=[bstats], writes=[bstats])
        p.op("te", lambda e: e.transpose(psb[2][0:HB, 0:128], stats[:, 0:HB], ident[:]),
             reads=[bstats], writes=[bpb[2]], extra=[bconst.w], inc=1)
        p.op("vt", lambda e: e.tensor_copy(statsT[0:HB, :], psb[2][0:HB, 0:128]),
             reads=[bpb[2]], writes=[bstatsT])
        p.dma("sy", negm_d.rearrange("o (a b) -> (o a) b", a=HB), statsT[0:HB, :],
              group="nmw", reads=[bstatsT], writes=[bnegm_d])
        rows = [(row[0:1, :], brow, "row"), (fbE[0:1, 0:512], brow2, "row2")]
        for c in range(HC):
            rv, rb, rg = rows[c % 2]
            p.dma("sy", rv, negm_d[:, c * 512:(c + 1) * 512], group=rg, extra=[bnegm_d.w], writes=[rb])
            p.op("te", lambda e, rv=rv: e.matmul(psb[3][:, :], ones1[:], rv, start=True, stop=True),
                 reads=[rb], writes=[bpb[3]], extra=[bconst.w], inc=1)
            p.op("vt", lambda e, c=c: e.tensor_copy(fbM[:, c * 512:(c + 1) * 512], psb[3][:, :]),
                 reads=[bpb[3]], writes=[bM])

    def rec_tt_dve(j0, j1):
        units = []
        n = 0
        for j in range(j0, j1):
            k = j % 2
            for hh in range(NEH):
                def load(j=j, hh=hh, n=n):
                    sl = slice(hh * EBC, (hh + 1) * EBC)
                    buf, bbuf = ebufs[n % 2]
                    p.dma("sy", buf, gfull_d[j * 128:(j + 1) * 128, sl], group="er%d" % (n % 2),
                          extra=[bgfull.w], writes=[bbuf])

                def work(j=j, k=k, hh=hh, n=n):
                    sl = slice(hh * EBC, (hh + 1) * EBC)
                    buf, bbuf = ebufs[n % 2]
                    p.op("gp", lambda e, sl=sl, buf=buf: e.tensor_tensor(buf, buf, fbM[:, sl], ALU.add),
                         reads=[bM], writes=[bbuf])
                    p.op("sc", lambda e, k=k, sl=sl, buf=buf: e.activation(tbuf[k][:, sl], buf, AF.Exp),
                         reads=[bbuf], writes=[btb[k]])
                    if hh == NEH - 1:
                        p.dma("sy", tt_d[j], tbuf[k], group="tw%d" % k, reads=[btb[k]], writes=[btt_d[j]])
                units.append((load, work))
                n += 1
        flat = []
        prev_work = None
        for (ld, wk) in units:
            if prev_work is None:
                flat.append(ld)
            else:
                flat.append(lambda ld=ld, pw=prev_work: (pw(), ld()))
            prev_work = wk
        flat.append(prev_work)
        return flat

    def rec_cw_prep(ex):
        kcw = ex % 2
        rows = [(row[0:1, :], brow, "row"), (fbE[0:1, 0:512], brow2, "row2")]
        for c in range(TCH):
            rv, rb, rg = rows[c % 2]
            p.dma("sy", rv, cwt_d[ex - 1:ex, c * 512:(c + 1) * 512], group=rg, extra=[bcwt_d.w], writes=[rb])
        for c in range(TCH):
            rv, rb, rg = rows[c % 2]
            p.op("te", lambda e, c=c, rv=rv: e.matmul(psb[2][:, :], ones1[:], rv, start=True, stop=True),
                 reads=[rb], writes=[bpb[2]], extra=[bconst.w], inc=1)
            p.op("vt", lambda e, c=c, kcw=kcw: e.tensor_copy(cwb[kcw][:, c * 512:(c + 1) * 512], psb[2][:, :]),
                 reads=[bpb[2]], writes=[bcw[kcw]])

    def rec_expert(ex, unit_hooks=None, mid_fn=None):
        is_shared = (ex == 0)
        kcw = ex % 2
        hooks = list(unit_hooks or [])
        per_strip = (len(hooks) + IB - 1) // IB if hooks else 0
        def rec_strip(i):
            kk = i % NSLOT
            p.dma("gp", wpair_view[kk][:].rearrange("p (s h) -> p s h", s=2),
                  gup_d[ex, i].rearrange("s p h -> p s h"),
                  group="wg%d" % kk, writes=[bwg[kk], bwu[kk]])
        for i in range(min(NSLOT, IB)):
            rec_strip(i)
        for i in range(IB):
            kk = i % NSLOT
            sA = 0 if (i % 2 == 0) else 4
            pg, bg_ = [psb[sA], psb[sA + 1]][:TCH], [bpb[sA], bpb[sA + 1]][:TCH]
            pu, bu_ = [psb[sA + 2], psb[sA + 3]][:TCH], [bpb[sA + 2], bpb[sA + 3]][:TCH]
            for sel, pp, bpp in ((0, pg, bg_), (1, pu, bu_)):
                wv, bb_ = ((wg_view, bwg) if sel == 0 else (wu_view, bwu))
                wv, bb_ = wv[kk], bb_[kk]
                for hb in range(HB):
                    p.op("te", lambda e, wv=wv, hb=hb: e.ldweights(wv[:, hb * 128:(hb + 1) * 128]),
                         extra=[bb_.w], inc=0)
                    for c in range(TCH):
                        first, lastm = (hb == 0), (hb == HB - 1)
                        p.op("te", lambda e, wv=wv, hb=hb, c=c, pp=pp, f=first, l=lastm: e.matmul(
                            pp[c][:], wv[:, hb * 128:(hb + 1) * 128], xt_h[hb][:, c * 512:(c + 1) * 512],
                            start=f, stop=l),
                            reads=([bb_, bxt[hb]] if lastm else []),
                            writes=([bpp[c]] if lastm else []),
                            war=([bpp[c]] if first else []),
                            extra=([bb_.w, bxt[hb].w] if not lastm else []),
                            inc=(1 if lastm else 0))
            for c in range(TCH):
                p.op("sc", lambda e, c=c, pg=pg: e.activation(loscr[:, c * 512:(c + 1) * 512], pg[c][:], AF.Silu),
                     reads=[bg_[c]], writes=[blo])
                p.op("vt", lambda e, c=c, i=i, pu=pu: e.tensor_tensor(
                    h_i[i][:, c * 512:(c + 1) * 512], loscr[:, c * 512:(c + 1) * 512], pu[c][:], ALU.mult),
                    reads=[blo, bu_[c]], writes=[bh[i]])
                if not is_shared:
                    p.op("vt", lambda e, c=c, i=i, kcw=kcw: e.tensor_tensor(
                        h_i[i][:, c * 512:(c + 1) * 512], h_i[i][:, c * 512:(c + 1) * 512],
                        cwb[kcw][:, c * 512:(c + 1) * 512], ALU.mult),
                        reads=[bcw[kcw]], writes=[bh[i]])
            if i + NSLOT < IB:
                rec_strip(i + NSLOT)
            if i == max(IB - NSLOT - 1, 0):
                dnt_toks[ex] = p.dma("gp", wbig[:, 0:IB * H].rearrange("p (i h) -> p i h", i=IB),
                                     dnt_d[ex].rearrange("i p h -> p i h"), group="dnt", writes=[bdt])
            for _ in range(per_strip):
                if hooks:
                    hooks.pop(0)()
        while hooks:
            hooks.pop(0)()
        if mid_fn is not None:
            mid_fn()
        last_down = None
        for t in range(TB):
            sA = 0 if (t % 2 == 0) else 4
            pd, bd_ = [psb[sA + c] for c in range(HC)], [bpb[sA + c] for c in range(HC)]
            for i in range(IB):
                p.op("te", lambda e, i=i, t=t: e.ldweights(h_i[i][:, t * 128:(t + 1) * 128]),
                     extra=[bh[i].w, dnt_toks[ex]], inc=0)
                for c in range(HC):
                    first, lastm = (i == 0), (i == IB - 1)
                    last_down = p.op("te", lambda e, i=i, c=c, t=t, pd=pd, f=first, l=lastm: e.matmul(
                        pd[c][:], h_i[i][:, t * 128:(t + 1) * 128], dt_i[i][:, c * 512:(c + 1) * 512],
                        start=f, stop=l),
                        reads=([bh[i], bdt] if lastm else []),
                        writes=([bd_[c]] if lastm else []),
                        war=([bd_[c]] if first else []),
                        extra=([bh[i].w, dnt_toks[ex]] if not lastm else []),
                        inc=(1 if lastm else 0))
            for c in range(HC):
                p.op("vt", lambda e, c=c, t=t, pd=pd: e.tensor_tensor(
                    acc_t[t][:, c * 512:(c + 1) * 512], acc_t[t][:, c * 512:(c + 1) * 512],
                    pd[c][:], ALU.add),
                    reads=[bd_[c]], writes=[bacc[t]])
        for i in range(IB):
            bh[i].rs.append(last_down)
        bdt.rs.append(last_down)

    stats_units = []
    tta_units = []
    ttb_units = []
    for ex in range(NE):
        hooks = None
        if ex == 3:
            stats_units = rec_stats_dve()
            hooks = stats_units[:len(stats_units) // 2]
        elif ex == 4:
            hooks = stats_units[len(stats_units) // 2:]
        elif ex == 6:
            tta_units = rec_tt_dve(0, HB // 2)
            hooks = tta_units[:len(tta_units) // 2]
        elif ex == 7:
            hooks = tta_units[len(tta_units) // 2:]
        elif ex == 8:
            ttb_units = rec_tt_dve(HB // 2, HB)
            hooks = ttb_units[:len(ttb_units) // 2]
        elif ex == 9:
            hooks = ttb_units[len(ttb_units) // 2:]
        mids = []
        if ex == 0:
            mids.append(rec_cwt)
            mids.append(lambda: rec_cw_prep(1))
        if 1 <= ex < NE - 1:
            mids.append(lambda ex=ex: rec_cw_prep(ex + 1))
        if ex == 5:
            mids.append(rec_stats_pe)
        mid = (lambda mids=mids: [m() for m in mids]) if mids else None
        rec_expert(ex, unit_hooks=hooks, mid_fn=mid)
        if ex == min(1, NE - 2):
            cc_tok = p.op("gp", lambda e: e.collective_compute(
                "AllReduce", ALU.add, replica_groups=[list(range(cfg.n_cores))],
                ins=[gpart_d], outs=[gfull_d]),
                extra=[bgpart.w], sem="cc", inc=1)
            bgfull.w = cc_tok

    PW = min(2, HC)
    bttrA = Buf("ttrA")
    bttrA.w = bdt.w
    bttrA.rs = list(bdt.rs)
    ttr2 = [wbig[:, j * (PW * 512):(j + 1) * (PW * 512)] for j in range(HB)]
    bttrB = [Buf(f"ttrB{q}") for q in range(4)]
    for q in range(4):
        bttrB[q].w = bdt.w
        bttrB[q].rs = list(bdt.rs)
    QS = max(HB // 4, 1)
    for pr in range(HC // PW):
        for j in range(HB):
            q = j // QS if QS else 0
            tok = p.dma("sc", ttr2[j], tt_d[j, :, pr * PW * 512:(pr + 1) * PW * 512],
                        group="ta%d" % min(q, 3), extra=[btt_d[HB - 1].w, btt_d[HB - 2].w],
                        writes=[bttrB[min(q, 3)]])
            if (j + 1) % QS == 0 or j == HB - 1:
                bttrA.w = tok
        for q in range(4):
            bttrB[min(q, 3)].w = bttrB[min(q, 3)].w
        p.op("te", lambda e: e.ldweights(ones128[:]), extra=[bconst.w], inc=0)
        for j in range(HB):
            q = min(j // QS, 3)
            first, lastm = (j == 0), (j == HB - 1)
            for c in range(PW):
                p.op("te", lambda e, j=j, c=c, f=first, l=lastm: e.matmul(
                    psb[2 + c][:], ones128[:], ttr2[j][:, c * 512:(c + 1) * 512], start=f, stop=l),
                    reads=([bttrB[q]] if lastm else []),
                    writes=([bpb[2 + c]] if lastm else []),
                    war=([bpb[2 + c]] if first else []),
                    extra=([bttrB[q].w, bconst.w] if not lastm else [bconst.w]),
                    inc=(1 if lastm else 0))
        for c in range(PW):
            ic = pr * PW + c
            p.op("vt", lambda e, ic=ic, c=c: e.reciprocal(fbM[:, ic * 512:(ic + 1) * 512], psb[2 + c][:]),
                 reads=[bpb[2 + c]], writes=[bM])
        for t in range(TB):
            s0 = 0 if t % 2 == 0 else 4
            banks = [psb[s0 + c] for c in range(PW)]
            bbs = [bpb[s0 + c] for c in range(PW)]
            for j in range(HB):
                q = min(j // QS, 3)
                p.op("te", lambda e, j=j, t=t: e.ldweights(xt_h[j][:, t * 128:(t + 1) * 128]),
                     extra=[bxt[j].w], inc=0)
                first, lastm = (j == 0), (j == HB - 1)
                for c in range(PW):
                    p.op("te", lambda e, j=j, t=t, c=c, bank=banks[c], f=first, l=lastm: e.matmul(
                        bank[:], xt_h[j][:, t * 128:(t + 1) * 128], ttr2[j][:, c * 512:(c + 1) * 512],
                        start=f, stop=l),
                        reads=([bttrB[q], bxt[j]] if lastm else []),
                        writes=([bbs[c]] if lastm else []),
                        war=([bbs[c]] if first else []),
                        extra=([bttrB[q].w, bxt[j].w] if not lastm else []),
                        inc=(1 if lastm else 0))
            for c in range(PW):
                ic = pr * PW + c
                p.op("vt", lambda e, ic=ic, bank=banks[c]: e.tensor_tensor(
                    fbE[:, 0:512], bank[:], fbM[:, ic * 512:(ic + 1) * 512], ALU.mult),
                    reads=[bbs[c], bM], writes=[bE])
                p.op("vt", lambda e, t=t, ic=ic: e.tensor_tensor(
                    acc_t[t][:, ic * 512:(ic + 1) * 512], acc_t[t][:, ic * 512:(ic + 1) * 512],
                    fbE[:, 0:512], ALU.add),
                    reads=[bE], writes=[bacc[t]])
        pair_end = bbs[PW - 1].w
        for q in range(4):
            if pair_end is not None:
                bttrB[q].rs.append(pair_end)

    out_tok = None
    for t in range(TB):
        out_tok = p.dma("sc", out_d[t * 128:(t + 1) * 128, :], acc_t[t], group="out", reads=[bacc[t]])
    p.wait("sc", out_tok)
    p.wait("sy", out_tok)

    block = es.enter_context(nc.Block())
    p.emit(block, es)
    es.close()
    return nc


def _prep_inputs(cfg, x, gate_w, ebias_np, gate_p, up_p, down_p, sh_g, sh_u, sh_d):
    H, I, TC, E, NC = cfg.H, cfg.I, cfg.TC, cfg.E, cfg.n_cores
    IB, HB = cfg.IB, cfg.HB
    x = np.ascontiguousarray(np.asarray(x, dtype=np.float32).reshape(-1, H))
    NE = E + 1

    def tile_gu(g, u):
        def t(a):
            return np.asarray(a, np.float32).reshape(IB, 128, HB, 128).transpose(0, 3, 2, 1).reshape(IB, 128, HB * 128)
        return np.stack([t(g), t(u)], axis=1)

    gup = np.empty((NE, IB, 2, 128, HB * 128), np.float32)
    gup[0] = tile_gu(sh_g, sh_u)
    for e in range(E):
        gup[e + 1] = tile_gu(gate_p[e], up_p[e])

    def tile_dn(d):
        return np.asarray(d, np.float32).T.reshape(IB, 128, H)

    dnt = np.empty((NE, IB, 128, H), np.float32)
    dnt[0] = tile_dn(sh_d)
    for e in range(E):
        dnt[e + 1] = tile_dn(down_p[e])

    wt = np.ascontiguousarray(np.asarray(gate_w, np.float32).T.reshape(HB, 128, E))
    eb = np.asarray(ebias_np, np.float32).reshape(1, E)
    ones1 = np.ones((1, 128), np.float32)
    ones128 = np.ones((128, 128), ml_dtypes.bfloat16)
    ident = np.eye(128, dtype=np.float32)

    shards = np.split(x, NC, axis=0)
    return [{"x": np.ascontiguousarray(shards[c]), "gup": gup, "dnt": dnt, "wt": wt, "eb": eb,
             "ones1": ones1, "ones128": ones128, "ident": ident} for c in range(NC)]


def kernel(L0_attn_x, L0_ffn_gate_weight, L0_ffn_expert_bias, L0_ffn_gate_proj,
           L0_ffn_up_proj, L0_ffn_down_proj, L0_ffn_shared_gate, L0_ffn_shared_up,
           L0_ffn_shared_down):
    cfg = CFG
    in_maps = _prep_inputs(cfg, L0_attn_x, L0_ffn_gate_weight, L0_ffn_expert_bias,
                           L0_ffn_gate_proj, L0_ffn_up_proj, L0_ffn_down_proj,
                           L0_ffn_shared_gate, L0_ffn_shared_up, L0_ffn_shared_down)
    nc = build_nc(cfg)
    res = run_bass_kernel_spmd(nc, in_maps, core_ids=list(range(cfg.n_cores)))
    out = np.concatenate([r["out"] for r in res.results], axis=0)
    return out.reshape(np.asarray(L0_attn_x).shape)
